# version 02 (final)
# baseline (speedup 1.0000x reference)
import numpy as np
import ml_dtypes
import concourse.bass as bass
import concourse.mybir as mybir
from concourse.bass_utils import run_bass_kernel_spmd
from concourse.alu_op_type import AluOpType as Alu
from contextlib import ExitStack
import bass_rust
from concourse import library_config

F32 = mybir.dt.float32
BF16 = mybir.dt.bfloat16
AF = mybir.ActivationFunctionType
AX = mybir.AxisListType
bf = ml_dtypes.bfloat16

NCORES = 8
SEQ = 2048
NSEQ = 2
T = SEQ * NSEQ
D = 1024
NT = T // 128
TPS = SEQ // 128
BIGM = float(2 ** 17)
BIGF = float(2 ** 22)
EPSA = 2.0 ** -20

STREAMS = ("pe", "act", "dve", "pool", "sp")
DMA_STREAMS = ("act", "pool", "sp")


class Op:
    __slots__ = ("stream", "kind", "emit", "seq", "g", "vc", "deps", "marked",
                 "tag", "psem", "pseq")

    def __init__(self, stream, kind, emit, seq, g, tag):
        self.stream = stream
        self.kind = kind
        self.emit = emit
        self.seq = seq
        self.g = g
        self.vc = None
        self.deps = []
        self.marked = False
        self.tag = tag
        self.psem = None
        self.pseq = None


NPOOL = 8


class Prog:
    def __init__(self, nc, es):
        self.nc = nc
        self.es = es
        self.ops = {s: [] for s in STREAMS}
        self.counts = {}
        self.bufs = {}
        self.sems = {}
        self.gctr = 0
        self.dmacnt = {s: 0 for s in DMA_STREAMS}
        self.last_on_sem = {}

    def _new(self, stream, kind, emit, tag):
        k = (stream, kind)
        seq = self.counts.get(k, 0)
        self.counts[k] = seq + 1
        op = Op(stream, kind, emit, seq, self.gctr, tag)
        self.gctr += 1
        self.ops[stream].append(op)
        return op

    def op(self, stream, emit, reads=(), writes=(), kind="eng", tag=None, extra_deps=()):
        o = self._new(stream, kind, emit, tag)
        deps, seen = [], set()

        def add(d):
            if d is not o and id(d) not in seen:
                seen.add(id(d))
                deps.append(d)

        for d in extra_deps:
            add(d)
        for key in reads:
            ent = self.bufs.get(key)
            if ent is not None and ent[0] is not None:
                add(ent[0])
        for key in writes:
            ent = self.bufs.get(key)
            if ent is not None:
                if ent[0] is not None:
                    add(ent[0])
                for r in ent[1]:
                    add(r)
        o.deps = deps
        for key in reads:
            self.bufs.setdefault(key, [None, []])[1].append(o)
        for key in writes:
            self.bufs[key] = [o, []]
        return o

    def dma(self, stream, out_ap, in_ap, reads=(), writes=(), tag=None):
        def emit(eng):
            return eng.dma_start(out_ap, in_ap)
        i = self.dmacnt[stream]
        self.dmacnt[stream] = i + 1
        psem = i % NPOOL
        prev = self.last_on_sem.get((stream, psem))
        o = self.op(stream, emit, reads=reads, writes=writes, kind="dma", tag=tag,
                    extra_deps=(prev,) if prev is not None else ())
        o.marked = True
        o.psem = psem
        o.pseq = i // NPOOL + 1
        self.last_on_sem[(stream, psem)] = o
        return o

    def barrier(self):
        lasts = []
        for s in STREAMS:
            for o in reversed(self.ops[s]):
                if o.tag != "barrier":
                    lasts.append(o)
                    break
        lasts.extend(o for o in self.last_on_sem.values()
                     if o is not None and o not in lasts)
        for s in STREAMS:
            o = self._new(s, "eng", None, "barrier")
            o.deps = list(lasts)
        self.bufs = {}

    def _dkey(self, o):
        if o.kind == "dma":
            return (o.stream, "dma", o.psem)
        return (o.stream, "eng")

    def _dseq(self, o):
        return o.pseq if o.kind == "dma" else o.seq

    def finalize(self, block):
        allops = sorted((o for s in STREAMS for o in self.ops[s]), key=lambda o: o.g)
        KEYS = [(s, "eng") for s in STREAMS] +                [(s, "dma", j) for s in DMA_STREAMS for j in range(NPOOL)]
        stream_vc = {s: {k: -1 for k in KEYS} for s in STREAMS}
        for o in allops:
            vc = dict(stream_vc[o.stream])
            needed = []
            for d in o.deps:
                dk = self._dkey(d)
                ds = self._dseq(d)
                if ds > vc[dk]:
                    needed.append(d)
                    d.marked = True
                    for k in KEYS:
                        if d.vc[k] > vc[k]:
                            vc[k] = d.vc[k]
                    if ds > vc[dk]:
                        vc[dk] = ds
            o.deps = needed
            o.vc = vc
            stream_vc[o.stream] = vc
        inc_at = {}
        for s in STREAMS:
            c, m = 0, {}
            for o in self.ops[s]:
                if o.kind != "eng":
                    continue
                if o.marked:
                    c += 1
                m[o.seq] = c
            inc_at[(s, "eng")] = m
        for s in STREAMS:
            self.sems[(s, "eng")] = self.es.enter_context(self.nc.semaphore(f"se_{s}"))
            if s in DMA_STREAMS:
                for j in range(NPOOL):
                    self.sems[(s, "dma", j)] = self.es.enter_context(
                        self.nc.semaphore(f"sd_{s}{j}"))
        engmap = {"pe": block.tensor, "act": block.scalar, "dve": block.vector,
                  "pool": block.gpsimd, "sp": block.sync}
        for s in STREAMS:
            ops_s = self.ops[s]
            if not ops_s:
                continue

            def make(ops_s=ops_s):
                def body(eng):
                    for o in ops_s:
                        for d in o.deps:
                            if d.kind == "dma":
                                sem = self.sems[(d.stream, "dma", d.psem)]
                                eng.wait_ge(sem, d.pseq * 16)
                            else:
                                sem = self.sems[(d.stream, "eng")]
                                eng.wait_ge(sem, inc_at[(d.stream, "eng")][d.seq])
                        if o.emit is None:
                            assert not o.marked
                            continue
                        inst = o.emit(eng)
                        if o.marked:
                            if o.kind == "dma":
                                inst.then_inc(self.sems[(o.stream, "dma", o.psem)], 16)
                            else:
                                assert inst is not None, f"unmarked-able op: {o.tag}"
                                inst.then_inc(self.sems[(o.stream, "eng")], 1)
                return body

            engmap[s](make())
        self._verify_incs()

    def _verify_incs(self):
        want = {}
        for s in STREAMS:
            n = sum(1 for o in self.ops[s] if o.kind == "eng" and o.marked)
            if n:
                want[(s, "eng")] = n
            if s in DMA_STREAMS:
                for j in range(NPOOL):
                    n = sum(1 for o in self.ops[s]
                            if o.kind == "dma" and o.psem == j)
                    if n:
                        want[(s, "dma", j)] = n * 16
        have = {}
        names = {}
        for key, sem in self.sems.items():
            names[sem.name if hasattr(sem, "name") else str(sem)] = key
        for inst in self.nc.all_instructions():
            si = inst.sync_info
            if si is None:
                continue
            u = si.on_update
            if u is None:
                continue
            ul = u if isinstance(u, (list, tuple)) else [u]
            for uu in ul:
                nm = getattr(uu, "ant_name", None) or ""
                if nm in names:
                    have[names[nm]] = have.get(names[nm], 0) + (uu.update_value or 1)
        for k, v in want.items():
            h = have.get(k, 0)
            assert h == v, f"semaphore inc mismatch for {k}: program has {h}, expects {v}"


def _host_consts():
    c = {}
    eye = np.eye(128, dtype=bf)
    c["ident"] = eye
    c["identf"] = np.eye(128, dtype=np.float32)
    ones = np.zeros((128, 128), dtype=np.float64)
    ones[:, :] = 1.0
    c["ones"] = ones.astype(bf)
    j = np.arange(128)
    c["mask_d"] = (j[:, None] <= j[None, :]).astype(bf)
    c["onesf"] = np.ones((128, 128), dtype=np.float32)
    c["gmask"] = (np.arange(128)[:, None] // 32 == np.arange(4)[None, :]).astype(np.float32)
    c["onesrow"] = np.ones((1, 128), dtype=bf)
    hd = 64
    inv_freq = 1.0 / (10000.0 ** (np.arange(0, hd, 2, dtype=np.float64) / hd))
    freqs = np.arange(SEQ, dtype=np.float64)[:, None] * inv_freq[None, :]
    cos = np.cos(freqs).astype(np.float32).reshape(TPS, 128, 32).transpose(1, 0, 2).reshape(128, TPS * 32)
    sin = np.sin(freqs).astype(np.float32).reshape(TPS, 128, 32).transpose(1, 0, 2).reshape(128, TPS * 32)
    c["rope_cos"] = np.ascontiguousarray(cos)
    c["rope_sin"] = np.ascontiguousarray(sin)
    tie = np.linspace(0.0, 1e-6, 32, dtype=np.float32)
    c["tie"] = np.tile(tie[None, :], (128, 1))
    cv = np.zeros((16,), dtype=np.float32)
    cv[0] = 1.0
    cv[1] = 0.5
    cv[2] = -0.5
    cv[3] = np.float32(EPSA - 1.0)
    cv[4] = 1e-6
    cv[5] = 1e-5
    cv[6] = 1.0 / 1024.0
    cv[7] = 1.0 / 4096.0
    cv[8] = 0.0
    cv[9] = -1.0
    cv[10] = 2.0
    c["cv"] = np.tile(cv[None, :], (128, 1))
    return c


def _prep_inputs(inputs):
    g = {}
    g["WqkvT0"] = np.ascontiguousarray(inputs["L0_attn_W_qkv"].T)
    g["WoutT0"] = np.ascontiguousarray(inputs["L0_attn_W_out"].T)
    g["hyp"] = np.ascontiguousarray(
        np.tile(inputs["L0_attn_hyp_k"].reshape(1, 16), (128, 1)))
    g["CTRL"] = np.ascontiguousarray(inputs["L0_ffn_controller"].reshape(1024, 128))
    f1r = inputs["L0_ffn_f1"].reshape(8, 128, 8, 16, 32)
    g["F1"] = np.ascontiguousarray(f1r.transpose(2, 1, 0, 3, 4).reshape(8, 128, 4096))
    f2 = inputs["L0_ffn_f2"].reshape(32, 128, 8, 128)
    g["F2"] = np.ascontiguousarray(f2.transpose(2, 1, 0, 3).reshape(8, 128, 32 * 128))
    g["FBIAS"] = np.ascontiguousarray(
        inputs["L0_ffn_bias"].reshape(128, 32).T)
    g["Wqkv1T"] = np.ascontiguousarray(inputs["L1_attn_w_qkv"].T)
    g["Wout1T"] = np.ascontiguousarray(inputs["L1_attn_w_out"].T)
    g["bout1"] = np.ascontiguousarray(inputs["L1_attn_b_out"].reshape(1, 1024))
    g["WfcT"] = np.ascontiguousarray(inputs["L1_ffn_W_fc"].T)
    g["WprojT"] = np.ascontiguousarray(inputs["L1_ffn_W_proj"].T)
    g["lnw"] = np.ascontiguousarray(inputs["L1_ffn_ln_w"].reshape(32, 128).T)
    g["lnb"] = np.ascontiguousarray(inputs["L1_ffn_ln_b"].reshape(32, 128).T)
    return g


def kernel(**inputs):
    x = np.ascontiguousarray(np.asarray(inputs["L0_attn_x"], dtype=np.float32))
    g = _prep_inputs({k: np.asarray(v, dtype=np.float32) for k, v in inputs.items()
                      if k != "L0_attn_x"})
    nc, innames = build_program()
    in_maps = []
    for c in range(NCORES):
        m = {"x_in": np.ascontiguousarray(x[2 * c:2 * c + 2].reshape(T, D))}
        for k in innames:
            m[k] = g[k]
        in_maps.append(m)
    res = run_bass_kernel_spmd(nc, in_maps, core_ids=list(range(NCORES)))
    out = np.concatenate([r["x_out"].reshape(2, SEQ, D) for r in res.results], axis=0)
    return out


IN_NAMES = ["WqkvT0", "WoutT0", "hyp", "CTRL", "F1", "F2", "FBIAS",
            "Wqkv1T", "Wout1T", "bout1", "WfcT", "WprojT", "lnw", "lnb"]

PARTS = (1, 2, 3, 4, 5, 6)
DEBUG_HA = False
BITFFN_TRUNC = None
MOE_TRUNC = None
ARENA0 = 16512
DYN0 = 26112


class Ctx:
    pass


def _alloc(C, name, shape, dt, off):
    t = C.nc.alloc_sbuf_tensor_at(name, shape, dt, offset=ARENA0 + off)
    C.base[t.name] = ARENA0 + off
    C.isz[t.name] = 4 if dt == F32 else 2
    return t


def ak(C, t, lo, hi):
    b = C.base[t.name]
    isz = C.isz[t.name]
    u0 = (b + lo * isz) >> 12
    u1 = (b + hi * isz - 1) >> 12
    return [("A", u) for u in range(u0, u1 + 1)]


def build_program(parts=PARTS):
    nc = bass.Bass("TRN2", target_bir_lowering=False)
    es = ExitStack()
    C = Ctx()
    C.nc, C.es = nc, es
    C.base, C.isz = {}, {}
    hc = _host_consts()

    C.x_in = nc.dram_tensor("x_in", [T, D], F32, kind="ExternalInput")
    C.x_out = nc.dram_tensor("x_out", [T, D], F32, kind="ExternalOutput")
    C.d_in = {}
    shapes = {"WqkvT0": [1024, 3072], "WoutT0": [1024, 1024], "hyp": [128, 16],
              "CTRL": [1024, 128], "F1": [8, 128, 128 * 32], "F2": [8, 128, 32 * 128],
              "FBIAS": [32, 128], "Wqkv1T": [1024, 1536], "Wout1T": [512, 1024],
              "bout1": [1, 1024], "WfcT": [1024, 4096], "WprojT": [4096, 1024],
              "lnw": [128, 32], "lnb": [128, 32]}
    for n in IN_NAMES:
        C.d_in[n] = nc.dram_tensor(n, shapes[n], F32, kind="ExternalInput")
    C.DBG = nc.dram_tensor("DBG", [128, 8192], F32)
    C.XA = nc.dram_tensor("XA", [T, D], F32)
    C.XB = nc.dram_tensor("XB", [T, D], F32)
    C.HSP = nc.dram_tensor("HSP", [NT, 32, 128, 128], BF16)
    C.WQF = nc.dram_tensor("WQF", [8, 128, 4096], BF16)
    C.WQP = nc.dram_tensor("WQP", [32, 128, 1024], BF16)
    C.ic = {}
    for n in ("ident", "identf", "ones", "onesf", "gmask", "mask_d", "onesrow",
              "rope_cos", "rope_sin", "tie", "cv"):
        C.ic[n] = nc.inline_tensor(hc[n], name="ic_" + n)

    C.ID = _alloc(C, "ID", [128, 128], BF16, 0)
    C.IDF = _alloc(C, "IDF", [128, 128], F32, 256)
    C.ONES = _alloc(C, "ONES", [128, 128], BF16, 768)
    C.ONESF = _alloc(C, "ONESF", [128, 128], F32, 1024)
    C.MD = _alloc(C, "MD", [128, 128], BF16, 25088)
    C.ONESROW = _alloc(C, "ONESROW", [1, 128], BF16, 1792)
    C.RC = _alloc(C, "RC", [128, 512], F32, 2048)
    C.RS = _alloc(C, "RS", [128, 512], F32, 4096)
    C.TIE = _alloc(C, "TIE", [128, 32], F32, 6144)
    C.CV = _alloc(C, "CV", [128, 16], F32, 6272)
    C.HYP = _alloc(C, "HYP", [128, 16], F32, 6336)
    C.HD = _alloc(C, "HD", [128, 128], F32, 6400)
    C.LNW = _alloc(C, "LNW", [128, 32], F32, 6912)
    C.LNB = _alloc(C, "LNB", [128, 32], F32, 7040)
    C.LNB16 = _alloc(C, "LNB16", [128, 32], BF16, 7168)
    C.GMASK = _alloc(C, "GMASK", [128, 4], F32, 25344)
    C.FB = _alloc(C, "FB", [32, 128], F32, 25408)
    C.CT = _alloc(C, "CT", [128, 1024], BF16, 7360)
    C.SMALL = _alloc(C, "SMALL", [128, 256], F32, 9408)
    C.TWN = _alloc(C, "TWN", [128, 128], BF16, 10432)
    C.ROW = _alloc(C, "ROW", [1, 2048], F32, 10688)
    C.ROW16 = _alloc(C, "ROW16", [1, 1024], BF16, 18880)
    C.AUX4 = _alloc(C, "AUX4", [128, 1024], F32, 20928)
    C.XW = _alloc(C, "XW", [128, 4096], F32, DYN0 + 0)
    C.XL = _alloc(C, "XL", [128, 4096], BF16, DYN0 + 16384)
    C.HB = _alloc(C, "HB", [128, 16384], BF16, DYN0 + 24576)
    C.WQ = _alloc(C, "WQ", [128, 32768], BF16, DYN0 + 57344)
    C.TRA = _alloc(C, "TRA", [128, 4096], BF16, DYN0 + 122880)
    C.TRB = _alloc(C, "TRB", [128, 4096], BF16, DYN0 + 131072)
    C.SCF = _alloc(C, "SCF", [128, 2048], F32, DYN0 + 139264)
    C.QSA = _alloc(C, "QSA", [128, 4096], F32, DYN0 + 147456)
    C.QSB = _alloc(C, "QSB", [128, 4096], F32, DYN0 + 163840)
    C.XM = _alloc(C, "XM", [128, 8192], F32, DYN0 + 0)
    C.XLM = _alloc(C, "XLM", [128, 8192], BF16, DYN0 + 32768)
    C.DT = _alloc(C, "DT", [128, 32768], BF16, DYN0 + 49152)
    C.S1A = _alloc(C, "S1A", [128, 4096], BF16, DYN0 + 114688)
    C.S1B = _alloc(C, "S1B", [128, 4096], BF16, DYN0 + 122880)
    C.S2Z = _alloc(C, "S2Z", [32, 16384], BF16, DYN0 + 131072)
    C.INR = _alloc(C, "INR", [32, 4096], BF16, DYN0 + 163840)
    C.LOGF = _alloc(C, "LOGF", [128, 1024], F32, DYN0 + 167936)
    C.PTB = _alloc(C, "PTB", [128, 1024], BF16, DYN0 + 172032)
    C.PTOK = _alloc(C, "PTOK", [128, 1024], BF16, DYN0 + 174080)
    C.XLT0 = _alloc(C, "XLT0", [128, 1024], BF16, DYN0 + 176128)
    C.SQM = _alloc(C, "SQM", [128, 4096], BF16, DYN0 + 178176)
    C.QF = _alloc(C, "QF", [128, 16384], BF16, DYN0 + 0)
    C.KF = _alloc(C, "KF", [128, 16384], BF16, DYN0 + 32768)
    C.VT = _alloc(C, "VT", [128, 16640], BF16, DYN0 + 65536)
    C.WQKV = _alloc(C, "WQKV", [128, 24576], BF16, DYN0 + 98816)
    C.AO = _alloc(C, "AO", [128, 16384], BF16, DYN0 + 98816)
    C.WOUT = _alloc(C, "WOUT", [128, 8192], BF16, DYN0 + 131584)
    C.SQA = _alloc(C, "SQA", [128, 4096], BF16, DYN0 + 174592)
    C.AXW = _alloc(C, "AXW", [128, 4096], F32, DYN0 + 147968)
    C.QROWS = _alloc(C, "QROWS", [16, 2048], F32, DYN0 + 147968)
    C.KROWS = _alloc(C, "KROWS", [16, 2048], F32, DYN0 + 156160)
    C.QRH = _alloc(C, "QRH", [1, 2048], F32, DYN0 + 131584)
    C.KRH = _alloc(C, "KRH", [1, 2048], F32, DYN0 + 139776)
    C.AXL = _alloc(C, "AXL", [128, 4096], BF16, DYN0 + 164352)
    C.AOC = _alloc(C, "AOC", [128, 1040], F32, DYN0 + 164352)
    C.AXT = _alloc(C, "AXT", [128, 1024], BF16, DYN0 + 172544)
    C.B1 = _alloc(C, "B1", [128, 1024], BF16, DYN0 + 174592)
    C.B2 = _alloc(C, "B2", [128, 1024], BF16, DYN0 + 176640)
    C.ET = _alloc(C, "ET", [128, 1024], BF16, DYN0 + 178688)
    C.BF = _alloc(C, "BF", [128, 1024], F32, DYN0 + 180736)
    C.QTM = _alloc(C, "QTM", [128, 1024], BF16, DYN0 + 174592)
    C.KTM = _alloc(C, "KTM", [128, 1024], BF16, DYN0 + 180736)
    C.RSC = _alloc(C, "RSC", [128, 2048], BF16, DYN0 + 176640)
    C.LQ = _alloc(C, "LQ", [128, 8192], BF16, DYN0 + 0)
    C.LK = _alloc(C, "LK", [128, 8192], BF16, DYN0 + 16384)
    C.LV = _alloc(C, "LV", [128, 8192], BF16, DYN0 + 32768)
    C.LAO = _alloc(C, "LAO", [128, 8192], BF16, DYN0 + 49152)
    C.LCTX = _alloc(C, "LCTX", [128, 512], BF16, DYN0 + 65536)
    C.LXW = _alloc(C, "LXW", [128, 8192], F32, DYN0 + 66560)
    C.LXL = _alloc(C, "LXL", [128, 8192], BF16, DYN0 + 99328)
    C.LXT0 = _alloc(C, "LXT0", [128, 1024], BF16, DYN0 + 115712)
    C.LXT1 = _alloc(C, "LXT1", [128, 1024], BF16, DYN0 + 117760)
    C.LSQ = _alloc(C, "LSQ", [128, 4096], BF16, DYN0 + 119808)
    C.LWQ = _alloc(C, "LWQ", [128, 12288], BF16, DYN0 + 128000)
    C.LWO = _alloc(C, "LWO", [128, 4096], BF16, DYN0 + 152576)

    C.PS = es.enter_context(nc.psum_tensor("PS", [128, 3072], F32))
    C.PSB = es.enter_context(nc.psum_tensor("PSB", [128, 2048], BF16))
    C.p = Prog(nc, es)
    C.cva = {"one": C.CV[:, 0:1], "half": C.CV[:, 1:2], "neghalf": C.CV[:, 2:3],
             "epsa1": C.CV[:, 3:4], "eps6": C.CV[:, 4:5], "eps5": C.CV[:, 5:6],
             "i1024": C.CV[:, 6:7], "i4096": C.CV[:, 7:8], "zero": C.CV[:, 8:9],
             "negone": C.CV[:, 9:10], "two": C.CV[:, 10:11]}

    with es:
        with nc.Block() as block:
            emit_all(C, parts)
            C.p.finalize(block)
    return nc, IN_NAMES


def bk(C, i, lo=0, n=512):
    return C.PS[:, 512 * i + lo: 512 * i + lo + n]


def bb(C, slot, n=128):
    col = (slot // 4) * 1024 + (slot % 4) * 128
    return C.PSB[:, col: col + n]


def pk(i):
    return [("PS", i)]


def pbk(slot):
    return [("PS", 6 + slot // 4)]


def emit_all(C, parts):
    p = C.p
    emit_preload(C)
    p.barrier()
    seqs = [(C.x_in, C.XA), (C.XA, C.XB), (C.XB, C.XA),
            (C.XA, C.XB), (C.XB, C.XA), (C.XA, C.x_out)]
    for i, (src, dst) in enumerate(seqs, start=1):
        if i in parts:
            if i in (1, 3):
                emit_moe(C, src, dst)
            elif i == 2:
                emit_hypattn(C, src, dst)
            elif i in (4, 6):
                emit_bitffn(C, src, dst, first=(i == 4))
            elif i == 5:
                emit_linattn(C, src, dst)
        else:
            emit_identity(C, src, dst)
        p.barrier()


def emit_identity(C, src, dst):
    p = C.p
    for mi in range(T // 1024):
        r = slice(mi * 1024, (mi + 1) * 1024)
        ap_in = src[r, :].rearrange("(a p) d -> p a d", p=128)
        ap_out = dst[r, :].rearrange("(a p) d -> p a d", p=128)
        x8 = C.XM[:, :].rearrange("p (a d) -> p a d", a=8)
        p.dma("sp", x8, ap_in, reads=[("D", src.name, mi)],
              writes=ak(C, C.XM, 0, 8192), tag="idin")
        p.dma("sp", ap_out, x8, reads=ak(C, C.XM, 0, 8192),
              writes=[("D", dst.name, mi)], tag="idout")


def emit_preload(C):
    p, hd, cv = C.p, C.HD, C.cva
    loads = [(C.ID, "ident"), (C.IDF, "identf"), (C.ONES, "ones"), (C.ONESF, "onesf"),
             (C.GMASK, "gmask"), (C.MD, "mask_d"), (C.ONESROW, "onesrow"), (C.RC, "rope_cos"),
             (C.RS, "rope_sin"), (C.TIE, "tie"), (C.CV, "cv")]
    for dstt, n in loads:
        srct = C.ic[n]
        sh = srct.shape
        p.dma("sp", dstt[0:sh[0], 0:sh[1]], srct[:, :],
              writes=ak(C, dstt, 0, sh[1]), tag="c_" + n)
    for dstt, n in [(C.HYP, "hyp"), (C.LNW, "lnw"), (C.LNB, "lnb"), (C.FB, "FBIAS")]:
        sh = dstt.shape
        p.dma("sp", dstt[:, :], C.d_in[n][:, :], writes=ak(C, dstt, 0, sh[1]), tag="c_" + n)
    p.dma("pool", C.CT[:, :].rearrange("p (c e) -> p c e", c=8),
          C.d_in["CTRL"][:, :].rearrange("(c p) e -> p c e", p=128),
          writes=ak(C, C.CT, 0, 1024), tag="ctrl_cast")
    p.op("dve", lambda v: v.reciprocal(hd[:, 64:96], C.LNW[:, :]),
         reads=ak(C, C.LNW, 0, 32), writes=[("hd", "wrec")])
    p.op("dve", lambda v: v.tensor_tensor(C.LNB16[:, :], C.LNB[:, :], hd[:, 64:96], Alu.mult),
         reads=ak(C, C.LNB, 0, 32) + [("hd", "wrec")], writes=ak(C, C.LNB16, 0, 32))
    p.op("act", lambda s: s.activation(hd[:, 48:64], C.HYP[:, :], AF.Ln),
         reads=ak(C, C.HYP, 0, 16), writes=ak(C, hd, 48, 64))
    p.op("act", lambda s: s.activation(hd[:, 0:16], hd[:, 48:64], AF.Exp, scale=cv["negone"]),
         reads=ak(C, hd, 48, 64) + ak(C, C.CV, 0, 16), writes=[("hd", "rinv")])
    p.op("act", lambda s: s.activation(hd[:, 16:32], hd[:, 48:64], AF.Exp, scale=cv["half"]),
         reads=ak(C, hd, 48, 64), writes=[("hd", "sig")])
    tw = C.TWN
    p.op("dve", lambda v: v.tensor_copy(tw[:, 0:16], C.ONES[:, 0:16]),
         reads=ak(C, C.ONES, 0, 16), writes=[("tw", "z")])
    p.op("dve", lambda v: v.tensor_tensor(tw[:, 16:32], tw[:, 0:16], tw[:, 0:16], Alu.mult),
         reads=[("tw", "z")], writes=[("tw", "s")])
    p.op("act", lambda s: s.activation(tw[:, 32:48], tw[:, 16:32], AF.Ln, bias=cv["epsa1"]),
         reads=[("tw", "s")], writes=[("tw", "a")])
    p.op("act", lambda s: s.activation(tw[:, 48:64], tw[:, 32:48], AF.Exp, scale=cv["half"]),
         reads=[("tw", "a")], writes=[("tw", "r")])
    p.op("dve", lambda v: v.tensor_tensor(tw[:, 64:80], tw[:, 0:16], tw[:, 48:64], Alu.add),
         reads=[("tw", "z"), ("tw", "r")], writes=[("tw", "u")])
    p.op("act", lambda s: s.activation(tw[:, 80:96], tw[:, 64:80], AF.Ln),
         reads=[("tw", "u")], writes=[("tw", "l")])
    for h in range(16):
        p.op("act", lambda s, h=h: s.activation(tw[:, 96 + h:97 + h], tw[:, 80 + h:81 + h],
                                                AF.Ln, scale=hd[:, 16 + h:17 + h],
                                                bias=cv["eps6"]),
             reads=[("tw", "l"), ("hd", "sig")], writes=[("tw", "b")])
    p.op("act", lambda s: s.activation(tw[:, 112:128], tw[:, 96:112], AF.Exp, scale=cv["negone"]),
         reads=[("tw", "b")], writes=[("tw", "w")])
    p.op("dve", lambda v: v.tensor_scalar(hd[:, 32:48], tw[:, 112:128], -1.0, None, Alu.mult),
         reads=[("tw", "w")], writes=[("hd", "nc")])


def emit_ln(C, src, t0, nsub, xw, xl, sqscr, tag):
    p, cv = C.p, C.cva
    S = C.SMALL
    ns = nsub
    r = slice(t0, t0 + 128 * nsub)
    ap_in = src[r, :].rearrange("(a p) d -> p a d", p=128)
    xwk = ak(C, xw, 0, nsub * 1024)
    p.dma("sp", xw[:, 0:nsub * 1024].rearrange("p (a d) -> p a d", a=nsub), ap_in,
          reads=[("D", src.name, t0 // 1024)], writes=xwk, tag=tag + "_ld")
    x3 = xw[:, 0:nsub * 1024].rearrange("p (a d) -> p a d", a=nsub)
    p.op("dve", lambda v: v.tensor_reduce(S[:, 0:ns], x3, AX.X, Alu.add),
         reads=xwk, writes=[("sm", 0)])
    for g0 in range(0, nsub, 4):
        gn = min(4, nsub - g0)
        p.op("act", lambda s, g0=g0, gn=gn: s.activation(
            sqscr[:, 0:gn * 1024], xw[:, g0 * 1024:(g0 + gn) * 1024], AF.Square),
            reads=xwk, writes=ak(C, sqscr, 0, gn * 1024), tag=tag + "_sq")
        p.op("dve", lambda v, g0=g0, gn=gn: v.tensor_reduce(
            S[:, ns + g0:ns + g0 + gn],
            sqscr[:, 0:gn * 1024].rearrange("p (a d) -> p a d", a=gn),
            AX.X, Alu.add), reads=ak(C, sqscr, 0, gn * 1024), writes=[("sm", 1)])
    p.op("dve", lambda v: v.tensor_scalar(S[:, 2 * ns:3 * ns], S[:, 0:ns],
                                          1.0 / 1024.0, None, Alu.mult),
         reads=[("sm", 0)], writes=[("sm", 2)])
    p.op("dve", lambda v: v.tensor_tensor(S[:, 0:ns], S[:, 2 * ns:3 * ns],
                                          S[:, 2 * ns:3 * ns], Alu.mult),
         reads=[("sm", 2)], writes=[("sm", 0)])
    p.op("dve", lambda v: v.scalar_tensor_tensor(S[:, 3 * ns:4 * ns], S[:, ns:2 * ns],
                                                 1.0 / 1024.0, S[:, 0:ns],
                                                 Alu.mult, Alu.subtract),
         reads=[("sm", 0), ("sm", 1)], writes=[("sm", 3)])
    p.op("act", lambda s: s.activation(S[:, ns:2 * ns], S[:, 3 * ns:4 * ns], AF.Ln,
                                       bias=cv["eps5"]),
         reads=[("sm", 3)], writes=[("sm", 1)], tag=tag + "_lnv")
    p.op("act", lambda s: s.activation(S[:, 4 * ns:5 * ns], S[:, ns:2 * ns], AF.Exp,
                                       scale=cv["neghalf"]),
         reads=[("sm", 1)], writes=[("sm", 4)], tag=tag + "_rst")
    for j in range(nsub):
        p.op("dve", lambda v, j=j: v.tensor_scalar(
            xl[:, j * 1024:(j + 1) * 1024], xw[:, j * 1024:(j + 1) * 1024],
            S[:, 2 * ns + j:2 * ns + j + 1], S[:, 4 * ns + j:4 * ns + j + 1],
            Alu.subtract, Alu.mult),
            reads=xwk + [("sm", 2), ("sm", 4)],
            writes=ak(C, xl, j * 1024, (j + 1) * 1024), tag=tag + "_ap")


def emit_quant(C, wsrc, wdst, nch, cols, fold, tag):
    p = C.p
    S = C.SMALL
    n_elem = nch * 128 * cols
    stage = [C.QSA, C.QSB]
    for c in range(nch):
        st = stage[c % 2]
        stk = ak(C, st, 0, cols)
        p.dma("sp", st[:, 0:cols], wsrc[c * 128:(c + 1) * 128, :],
              writes=stk, tag=tag + "_qld")
        p.op("dve", lambda v, st=st: v.tensor_reduce(
            S[:, 65:66], st[:, 0:cols], AX.X, Alu.add, apply_absolute_value=True),
            reads=stk, writes=[("qs", "t")], tag=tag + "_qrd")
        if c == 0:
            p.op("dve", lambda v: v.tensor_copy(S[:, 64:65], S[:, 65:66]),
                 reads=[("qs", "t")], writes=[("qs", "run")])
        else:
            p.op("dve", lambda v: v.tensor_tensor(S[:, 64:65], S[:, 64:65],
                                                  S[:, 65:66], Alu.add),
                 reads=[("qs", "t"), ("qs", "run")], writes=[("qs", "run")])
    p.op("pe", lambda t: t.matmul(C.PS[:, 2560:2561], C.ONESF[:, :], S[:, 64:65],
                                  start=True, stop=True, skip_group_check=True),
         reads=[("qs", "run")] + ak(C, C.ONESF, 0, 128), writes=pk(5),
         tag=tag + "_par")
    p.op("dve", lambda v: v.tensor_copy(S[:, 68:69], C.PS[:, 2560:2561]),
         reads=pk(5), writes=[("qs", "tot")], tag=tag + "_pare")
    p.op("dve", lambda v: v.tensor_scalar(S[:, 66:67], S[:, 68:69],
                                          0.5 / n_elem, 0.5e-6, Alu.mult, Alu.add),
         reads=[("qs", "tot")], writes=[("qs", "thb")])
    p.op("dve", lambda v: v.tensor_scalar(S[:, 67:68], S[:, 66:67],
                                          -1.0, None, Alu.mult),
         reads=[("qs", "thb")], writes=[("qs", "nthb")])
    for c in range(nch):
        st = stage[c % 2]
        stk = ak(C, st, 0, cols)
        p.dma("sp", st[:, 0:cols], wsrc[c * 128:(c + 1) * 128, :],
              writes=stk, tag=tag + "_qld2")
        p.op("dve", lambda v, st=st: v.tensor_scalar(
            C.TRA[:, 0:cols], st[:, 0:cols], S[:, 66:67], None, Alu.is_gt),
            reads=stk + [("qs", "thb")], writes=ak(C, C.TRA, 0, cols))
        p.op("dve", lambda v, st=st: v.tensor_scalar(
            C.TRB[:, 0:cols], st[:, 0:cols], S[:, 67:68], None, Alu.is_lt),
            reads=stk + [("qs", "nthb")], writes=ak(C, C.TRB, 0, cols))
        p.op("dve", lambda v: v.tensor_tensor(C.TRA[:, 0:cols], C.TRA[:, 0:cols],
                                              C.TRB[:, 0:cols], Alu.subtract),
             reads=ak(C, C.TRA, 0, cols) + ak(C, C.TRB, 0, cols),
             writes=ak(C, C.TRA, 0, cols))
        if fold is not None:
            p.op("dve", lambda v, c=c: v.tensor_scalar(
                C.TRA[:, 0:cols], C.TRA[:, 0:cols], fold[:, c:c + 1], None,
                Alu.mult),
                reads=ak(C, C.TRA, 0, cols) + ak(C, fold, c, c + 1),
                writes=ak(C, C.TRA, 0, cols))
        p.dma("sp", wdst[c, :, :], C.TRA[:, 0:cols],
              reads=ak(C, C.TRA, 0, cols), writes=[("D", wdst.name, c)],
              tag=tag + "_qst")


def emit_bitffn(C, src, dst, first):
    p, cv = C.p, C.cva
    S = C.SMALL
    if first:
        emit_quant(C, C.d_in["WfcT"], C.WQF, 8, 4096, None, "qf")
        emit_quant(C, C.d_in["WprojT"], C.WQP, 32, 1024, C.LNW, "qp")
        p.barrier()
    if BITFFN_TRUNC == "quant":
        emit_identity(C, src, dst)
        return
    wqk = ak(C, C.WQ, 0, 32768)
    p.dma("sp", C.WQ[:, :].rearrange("p (c h) -> p c h", c=8),
          C.WQF[:, :, :].rearrange("c p h -> p c h"),
          reads=[("D", C.WQF.name, c) for c in range(8)], writes=wqk, tag="wqf_ld")
    for mini in range(T // 512):
        t0 = mini * 512
        emit_ln(C, src, t0, 4, C.XW, C.XL, C.TRA, f"bf{mini}")
        if BITFFN_TRUNC == "ln":
            continue
        for j in range(4):
            for c in range(8):
                sl = c % 8
                p.op("pe", lambda t, j=j, c=c, sl=sl: t.transpose(
                    bb(C, sl),
                    C.XL[:, j * 1024 + c * 128: j * 1024 + (c + 1) * 128],
                    C.ID[:, :]),
                    reads=ak(C, C.XL, j * 1024 + c * 128, j * 1024 + (c + 1) * 128)
                    + ak(C, C.ID, 0, 128), writes=[("PSB", sl)] + pbk(sl), tag="bf_xt")
                p.op("dve", lambda v, j=j, c=c, sl=sl: v.tensor_copy(
                    C.TRA[:, j * 1024 + c * 128: j * 1024 + (c + 1) * 128],
                    bb(C, sl)),
                    reads=[("PSB", sl)] + pbk(sl),
                    writes=ak(C, C.TRA, j * 1024 + c * 128, j * 1024 + (c + 1) * 128))
        for j in range(4):
            for n in range(8):
                b = 2 + (n % 2)
                def mmf(t, j=j, n=n, b=b):
                    last = None
                    for c in range(8):
                        last = t.matmul(bk(C, b), C.TRA[:, j * 1024 + c * 128: j * 1024 + (c + 1) * 128],
                                        C.WQ[:, c * 4096 + n * 512: c * 4096 + (n + 1) * 512],
                                        start=(c == 0), stop=(c == 7))
                    return last
                p.op("pe", mmf, reads=ak(C, C.TRA, j * 1024, (j + 1) * 1024) + wqk,
                     writes=pk(b), tag="bf_fc")
                p.op("act", lambda s, j=j, n=n, b=b: s.activation(
                    C.HB[:, j * 4096 + n * 512: j * 4096 + (n + 1) * 512],
                    bk(C, b), AF.Gelu),
                    reads=pk(b),
                    writes=ak(C, C.HB, j * 4096 + n * 512, j * 4096 + (n + 1) * 512),
                    tag="bf_gelu")
        if BITFFN_TRUNC == "mm":
            continue
        for j in range(4):
            hk = ak(C, C.HB, j * 4096, (j + 1) * 4096)
            h3 = C.HB[:, j * 4096:(j + 1) * 4096].rearrange("p (a d) -> p a d", a=1)
            p.op("dve", lambda v, j=j, h3=h3: v.tensor_reduce(
                S[:, 128 + j:128 + j + 1], h3, AX.X, Alu.add),
                reads=hk, writes=[("ism", 0, j)])
            p.op("act", lambda s_, j=j: s_.activation(
                C.TRB[:, 0:4096], C.HB[:, j * 4096:(j + 1) * 4096], AF.Square,
                accum_out=S[:, 132 + j:132 + j + 1]),
                reads=hk, writes=[("ism", 1, j)] + ak(C, C.TRB, 0, 4096), tag="bf_sq")
        p.op("dve", lambda v: v.tensor_scalar(S[:, 136:140], S[:, 128:132],
                                              1.0 / 4096.0, None, Alu.mult),
             reads=[("ism", 0, j) for j in range(4)], writes=[("ism", "mean")])
        p.op("dve", lambda v: v.tensor_tensor(S[:, 140:144], S[:, 136:140],
                                              S[:, 136:140], Alu.mult),
             reads=[("ism", "mean")], writes=[("ism", "m2")])
        p.op("dve", lambda v: v.scalar_tensor_tensor(S[:, 144:148], S[:, 132:136],
                                                     1.0 / 4096.0, S[:, 140:144],
                                                     Alu.mult, Alu.subtract),
             reads=[("ism", "m2")] + [("ism", 1, j) for j in range(4)],
             writes=[("ism", "var")])
        p.op("act", lambda s: s.activation(S[:, 140:144], S[:, 144:148], AF.Ln,
                                           bias=cv["eps5"]),
             reads=[("ism", "var")], writes=[("ism", "lv")], tag="bf_ilv")
        p.op("act", lambda s: s.activation(S[:, 148:152], S[:, 140:144], AF.Exp,
                                           scale=cv["neghalf"]),
             reads=[("ism", "lv")], writes=[("ism", "rstd")], tag="bf_irs")
        for j in range(4):
            hk = ak(C, C.HB, j * 4096, (j + 1) * 4096)
            p.op("dve", lambda v, j=j: v.tensor_scalar(
                C.HB[:, j * 4096:(j + 1) * 4096], C.HB[:, j * 4096:(j + 1) * 4096],
                S[:, 136 + j:136 + j + 1], S[:, 148 + j:148 + j + 1],
                Alu.subtract, Alu.mult),
                reads=hk + [("ism", "mean"), ("ism", "rstd")], writes=hk, tag="bf_iap")
        if BITFFN_TRUNC == "iln":
            continue
        for j in range(4):
            tile = (t0 // 128) + j
            for c in range(32):
                sl = c % 8
                p.op("pe", lambda t, j=j, c=c, sl=sl: t.transpose(
                    bb(C, sl),
                    C.HB[:, j * 4096 + c * 128: j * 4096 + (c + 1) * 128],
                    C.ID[:, :]),
                    reads=ak(C, C.HB, j * 4096 + c * 128, j * 4096 + (c + 1) * 128)
                    + ak(C, C.ID, 0, 128), writes=[("PSB", sl)] + pbk(sl), tag="bf_ht")
                p.op("dve", lambda v, c=c, sl=sl: v.tensor_copy(
                    C.TRB[:, c * 128:(c + 1) * 128],
                    bb(C, sl)),
                    reads=[("PSB", sl)] + pbk(sl),
                    writes=ak(C, C.TRB, c * 128, (c + 1) * 128))
            if BITFFN_TRUNC != "nospill":
                p.dma("sp", C.HSP[tile, :, :, :].rearrange("c p t -> p c t"),
                      C.TRB[:, :].rearrange("p (c t) -> p c t", c=32),
                      reads=ak(C, C.TRB, 0, 4096), writes=[("D", "HSP", tile)],
                      tag="bf_spill")
    p.barrier()
    if BITFFN_TRUNC in ("ln", "mm", "iln", "fc", "nospill"):
        emit_identity(C, src, dst)
        return
    p.dma("sp", C.WQ[:, :].rearrange("p (c h) -> p c h", c=32),
          C.WQP[:, :, :].rearrange("c p h -> p c h"),
          reads=[("D", C.WQP.name, c) for c in range(32)], writes=wqk, tag="wqp_ld")
    for half in range(2):
        def mmb(t, half=half):
            last = None
            for c in range(32):
                last = t.matmul(C.PS[0:1, (4 + half) * 512:(5 + half) * 512],
                                C.LNB16[:, c:c + 1],
                                C.WQ[:, c * 1024 + half * 512: c * 1024 + (half + 1) * 512],
                                start=(c == 0), stop=(c == 31))
            return last
        p.op("pe", mmb, reads=wqk + ak(C, C.LNB16, 0, 32), writes=pk(4 + half),
             tag="brow_mm")
    p.op("dve", lambda v: v.tensor_copy(C.ROW[0:1, 0:512], C.PS[0:1, 2048:2560]),
         reads=pk(4), writes=[("br", 0)])
    p.op("dve", lambda v: v.tensor_copy(C.ROW[0:1, 512:1024], C.PS[0:1, 2560:3072]),
         reads=pk(5), writes=[("br", 1)])
    for i in range(2):
        p.op("pe", lambda t, i=i: t.matmul(
            bk(C, 4 + i), C.ONESF[0:1, 0:128], C.ROW[0:1, i * 512:(i + 1) * 512],
            start=True, stop=True, skip_group_check=True),
            reads=[("br", i)] + ak(C, C.ONESF, 0, 128), writes=pk(4 + i),
            tag="brow_bc")
        p.op("dve", lambda v, i=i: v.tensor_scalar(
            C.SCF[:, i * 512:(i + 1) * 512], bk(C, 4 + i), 0.5, None, Alu.mult),
            reads=pk(4 + i), writes=ak(C, C.SCF, i * 512, (i + 1) * 512),
            tag="brow_sc")
    for mini in range(T // 512):
        t0 = mini * 512
        r = slice(t0, t0 + 512)
        xwk = ak(C, C.XW, 0, 4096)
        p.dma("sp", C.XW[:, :].rearrange("p (a d) -> p a d", a=4),
              src[r, :].rearrange("(a p) d -> p a d", p=128),
              reads=[("D", src.name, "p", mini)], writes=xwk, tag="pj_xld")
        for j in range(4):
            tile = (t0 // 128) + j
            p.dma("sp", C.TRA[:, :].rearrange("p (c t) -> p c t", c=32),
                  C.HSP[tile, :, :, :].rearrange("c p t -> p c t"),
                  reads=[("D", "HSP", tile)], writes=ak(C, C.TRA, 0, 4096),
                  tag="pj_hld")
            for n in range(2):
                b = 4 + n
                def mmp(t, n=n, b=b):
                    last = None
                    for c in range(32):
                        last = t.matmul(bk(C, b), C.TRA[:, c * 128:(c + 1) * 128],
                                        C.WQ[:, c * 1024 + n * 512: c * 1024 + (n + 1) * 512],
                                        start=(c == 0), stop=(c == 31))
                    return last
                p.op("pe", mmp, reads=ak(C, C.TRA, 0, 4096) + wqk, writes=pk(b),
                     tag="pj_mm")
                xsk = ak(C, C.XW, j * 1024 + n * 512, j * 1024 + (n + 1) * 512)
                p.op("dve", lambda v, j=j, n=n, b=b: v.scalar_tensor_tensor(
                    C.XW[:, j * 1024 + n * 512: j * 1024 + (n + 1) * 512],
                    bk(C, b), 0.5, C.XW[:, j * 1024 + n * 512: j * 1024 + (n + 1) * 512],
                    Alu.mult, Alu.add),
                    reads=pk(b) + xsk, writes=xsk, tag="pj_res1")
                p.op("dve", lambda v, j=j, n=n: v.tensor_tensor(
                    C.XW[:, j * 1024 + n * 512: j * 1024 + (n + 1) * 512],
                    C.XW[:, j * 1024 + n * 512: j * 1024 + (n + 1) * 512],
                    C.SCF[:, n * 512:(n + 1) * 512], Alu.add),
                    reads=xsk + ak(C, C.SCF, n * 512, (n + 1) * 512),
                    writes=xsk, tag="pj_res2")
        p.dma("sp", dst[r, :].rearrange("(a p) d -> p a d", p=128),
              C.XW[:, :].rearrange("p (a d) -> p a d", a=4),
              reads=ak(C, C.XW, 0, 4096), writes=[("D", dst.name, "o", mini)],
              tag="pj_xst")


def emit_linattn(C, src, dst):
    p, cv, S = C.p, C.cva, C.SMALL
    wq = ak(C, C.LWQ, 0, 12288)
    wo = ak(C, C.LWO, 0, 4096)
    p.dma("pool", C.LWQ[:, :].rearrange("p (c f) -> p c f", c=8),
          C.d_in["Wqkv1T"][:, :].rearrange("(c p) f -> p c f", p=128),
          writes=wq, tag="la_wq")
    p.dma("pool", C.LWO[:, :].rearrange("p (c f) -> p c f", c=4),
          C.d_in["Wout1T"][:, :].rearrange("(c p) f -> p c f", p=128),
          writes=wo, tag="la_wo")
    p.dma("sp", C.ROW[0:1, 0:1024], C.d_in["bout1"][0:1, :],
          writes=[("la", "bo")], tag="la_bo")
    p.op("dve", lambda v: v.tensor_copy(C.ROW16[0:1, 0:1024], C.ROW[0:1, 0:1024]),
         reads=[("la", "bo")], writes=[("la", "bo16")])
    for s in range(NSEQ):
        for mac in range(2):
            t0 = s * SEQ + mac * 1024
            emit_ln(C, src, t0, 8, C.LXW, C.LXL, C.LSQ, f"la{s}{mac}")
            for j in range(8):
                sub = mac * 8 + j
                xk = ak(C, C.LXL, j * 1024, (j + 1) * 1024)
                for c in range(8):
                    sl = c % 8
                    p.op("pe", lambda t, j=j, c=c, sl=sl: t.transpose(
                        bb(C, sl), C.LXL[:, j * 1024 + c * 128: j * 1024 + (c + 1) * 128],
                        C.ID[:, :]),
                        reads=ak(C, C.LXL, j * 1024 + c * 128, j * 1024 + (c + 1) * 128),
                        writes=[("PSB", sl)] + pbk(sl), tag="la_xt")
                    p.op("dve", lambda v, c=c, sl=sl: v.tensor_copy(
                        C.LXT0[:, c * 128:(c + 1) * 128], bb(C, sl)),
                        reads=[("PSB", sl)] + pbk(sl),
                        writes=ak(C, C.LXT0, c * 128, (c + 1) * 128))
                xtk = ak(C, C.LXT0, 0, 1024)
                def mmk(t):
                    last = None
                    for c in range(8):
                        last = t.matmul(bk(C, 0), C.LXT0[:, c * 128:(c + 1) * 128],
                                        C.LWQ[:, c * 1536 + 512: c * 1536 + 1024],
                                        start=(c == 0), stop=(c == 7))
                    return last
                p.op("pe", mmk, reads=xtk + wq, writes=pk(0), tag="la_kmm")
                kk = ak(C, C.LK, sub * 512, (sub + 1) * 512)
                p.op("act", lambda a, sub=sub: a.activation(
                    C.LK[:, sub * 512:(sub + 1) * 512], bk(C, 0), AF.Exp),
                    reads=pk(0), writes=kk, tag="la_kexp")
                p.op("dve", lambda v, sub=sub: v.tensor_reduce(
                    S[:, 32:40], C.LK[:, sub * 512:(sub + 1) * 512]
                    .rearrange("p (h d) -> p h d", h=8), AX.X, Alu.add),
                    reads=kk, writes=[("la", "ks", 0)])
                p.op("dve", lambda v: v.reciprocal(S[:, 40:48], S[:, 32:40]),
                     reads=[("la", "ks", 0)], writes=[("la", "ks", 1)])
                p.op("dve", lambda v, sub=sub: v.tensor_tensor(
                    C.LK[:, sub * 512:(sub + 1) * 512].rearrange("p (h d) -> p h d", h=8),
                    C.LK[:, sub * 512:(sub + 1) * 512].rearrange("p (h d) -> p h d", h=8),
                    bass.AP(S, 40, [[256, 128], [1, 8], [0, 64]]), Alu.mult),
                    reads=kk + [("la", "ks", 1)], writes=kk, tag="la_knorm")
                def mmv(t):
                    last = None
                    for c in range(8):
                        last = t.matmul(bk(C, 1), C.LXT0[:, c * 128:(c + 1) * 128],
                                        C.LWQ[:, c * 1536 + 1024: c * 1536 + 1536],
                                        start=(c == 0), stop=(c == 7))
                    return last
                p.op("pe", mmv, reads=xtk + wq, writes=pk(1), tag="la_vmm")
                p.op("dve", lambda v, sub=sub: v.tensor_copy(
                    C.LV[:, sub * 512:(sub + 1) * 512], bk(C, 1)),
                    reads=pk(1), writes=ak(C, C.LV, sub * 512, (sub + 1) * 512))
                def mmq(t, j=j):
                    last = None
                    for fb in range(4):
                        for c in range(8):
                            last = t.matmul(
                                bk(C, 2, fb * 128, 128),
                                C.LWQ[:, c * 1536 + fb * 128: c * 1536 + (fb + 1) * 128],
                                C.LXT0[:, c * 128:(c + 1) * 128],
                                start=(c == 0), stop=(c == 7))
                    return last
                p.op("pe", mmq, reads=xtk + wq, writes=pk(2), tag="la_qmm")
                for fb in range(4):
                    p.op("dve", lambda v, sub=sub, fb=fb: v.tensor_copy(
                        C.LQ[:, fb * 2048 + sub * 128: fb * 2048 + (sub + 1) * 128],
                        bk(C, 2, fb * 128, 128)),
                        reads=pk(2),
                        writes=ak(C, C.LQ, fb * 2048 + sub * 128, fb * 2048 + (sub + 1) * 128))
        for pb in range(4):
            def mmc(t, pb=pb, s=s):
                last = None
                for hh in range(2):
                    h = pb * 2 + hh
                    for n in range(16):
                        last = t.matmul(
                            C.PS[hh * 64:(hh + 1) * 64, 1536 + pb * 64: 1536 + (pb + 1) * 64],
                            C.LK[:, n * 512 + h * 64: n * 512 + (h + 1) * 64],
                            C.LV[:, n * 512 + h * 64: n * 512 + (h + 1) * 64],
                            start=(n == 0), stop=(n == 15),
                            tile_position=(0, hh * 64))
                return last
            p.op("pe", mmc, reads=ak(C, C.LK, 0, 8192) + ak(C, C.LV, 0, 8192),
                 writes=pk(3), tag="la_ctx")
            p.op("dve", lambda v, pb=pb: v.tensor_copy(
                C.LCTX[:, pb * 64:(pb + 1) * 64], C.PS[:, 1536 + pb * 64: 1536 + (pb + 1) * 64]),
                reads=pk(3), writes=ak(C, C.LCTX, pb * 64, (pb + 1) * 64))
        for pb in range(4):
            for tb in range(4):
                def mmo(t, pb=pb, tb=tb):
                    last = None
                    for hh in range(2):
                        last = t.matmul(
                            C.PS[hh * 64:(hh + 1) * 64, 2048 + tb % 2 * 512: 2048 + (tb % 2) * 512 + 512],
                            C.LCTX[hh * 64:(hh + 1) * 64, pb * 64:(pb + 1) * 64],
                            C.LQ[hh * 64:(hh + 1) * 64, pb * 2048 + tb * 512: pb * 2048 + (tb + 1) * 512],
                            start=True, stop=True,
                            tile_position=(hh * 64, hh * 64))
                    return last
                b = 4 + tb % 2
                p.op("pe", mmo, reads=ak(C, C.LCTX, pb * 64, (pb + 1) * 64)
                     + ak(C, C.LQ, pb * 2048 + tb * 512, pb * 2048 + (tb + 1) * 512),
                     writes=pk(b), tag="la_qctx")
                p.op("dve", lambda v, pb=pb, tb=tb, b=b: v.tensor_copy(
                    C.LAO[:, pb * 2048 + tb * 512: pb * 2048 + (tb + 1) * 512], bk(C, b)),
                    reads=pk(b),
                    writes=ak(C, C.LAO, pb * 2048 + tb * 512, pb * 2048 + (tb + 1) * 512))
        for mac in range(2):
            t0 = s * SEQ + mac * 1024
            r = slice(t0, t0 + 1024)
            xwk = ak(C, C.LXW, 0, 8192)
            p.dma("sp", C.LXW[:, :].rearrange("p (a d) -> p a d", a=8),
                  src[r, :].rearrange("(a p) d -> p a d", p=128),
                  reads=[("D", src.name, "r", t0)], writes=xwk, tag="la_xld")
            for j in range(8):
                sub = mac * 8 + j
                for n in range(2):
                    b = n
                    def mmp(t, sub=sub, n=n, b=b):
                        last = t.matmul(bk(C, b), C.ONESROW[0:1, :],
                                        C.ROW16[0:1, n * 512:(n + 1) * 512],
                                        start=True, stop=False)
                        for pb in range(4):
                            last = t.matmul(bk(C, b),
                                            C.LAO[:, pb * 2048 + sub * 128: pb * 2048 + (sub + 1) * 128],
                                            C.LWO[:, pb * 1024 + n * 512: pb * 1024 + (n + 1) * 512],
                                            start=False, stop=(pb == 3))
                        return last
                    p.op("pe", mmp, reads=ak(C, C.LAO, 0, 8192) + wo + [("la", "bo16")],
                         writes=pk(b), tag="la_pmm")
                    xsk = ak(C, C.LXW, j * 1024 + n * 512, j * 1024 + (n + 1) * 512)
                    p.op("dve", lambda v, j=j, n=n, b=b: v.tensor_tensor(
                        C.LXW[:, j * 1024 + n * 512: j * 1024 + (n + 1) * 512],
                        C.LXW[:, j * 1024 + n * 512: j * 1024 + (n + 1) * 512],
                        bk(C, b), Alu.add),
                        reads=pk(b) + xsk, writes=xsk, tag="la_res")
            p.dma("sp", dst[r, :].rearrange("(a p) d -> p a d", p=128),
                  C.LXW[:, :].rearrange("p (a d) -> p a d", a=8),
                  reads=xwk, writes=[("D", dst.name, "w", t0)], tag="la_xst")


def emit_moe(C, src, dst):
    p, cv, S = C.p, C.cva, C.SMALL
    ctk = ak(C, C.CT, 0, 1024)
    for mi in range(4):
        t0 = mi * 1024
        emit_ln(C, src, t0, 8, C.XM, C.XLM, C.SQM, f"mo{mi}")
        for j in range(8):
            for c in range(8):
                sl = c % 8
                p.op("pe", lambda t, j=j, c=c, sl=sl: t.transpose(
                    bb(C, sl), C.XLM[:, j * 1024 + c * 128: j * 1024 + (c + 1) * 128],
                    C.ID[:, :]),
                    reads=ak(C, C.XLM, j * 1024 + c * 128, j * 1024 + (c + 1) * 128),
                    writes=[("PSB", sl)] + pbk(sl), tag="mo_xt")
                p.op("dve", lambda v, c=c, sl=sl: v.tensor_copy(
                    C.XLT0[:, c * 128:(c + 1) * 128], bb(C, sl)),
                    reads=[("PSB", sl)] + pbk(sl),
                    writes=ak(C, C.XLT0, c * 128, (c + 1) * 128))
            def mml(t):
                last = None
                for c in range(8):
                    last = t.matmul(bk(C, 4, 0, 128), C.CT[:, c * 128:(c + 1) * 128],
                                    C.XLT0[:, c * 128:(c + 1) * 128],
                                    start=(c == 0), stop=(c == 7))
                return last
            p.op("pe", mml, reads=ak(C, C.XLT0, 0, 1024) + ctk, writes=pk(4), tag="mo_lg")
            lk = ak(C, C.LOGF, j * 128, (j + 1) * 128)
            p.op("dve", lambda v, j=j: v.tensor_tensor(
                C.LOGF[:, j * 128:(j + 1) * 128].rearrange("p (g t) -> p g t", g=4),
                bk(C, 4, 0, 128).rearrange("p (g t) -> p g t", g=4),
                bass.AP(C.TIE, 0, [[32, 128], [0, 4], [1, 32]]), Alu.add),
                reads=pk(4) + ak(C, C.TIE, 0, 32), writes=lk, tag="mo_tie")
            p.op("dve", lambda v, j=j: v.tensor_reduce(
                S[:, 56:60], C.LOGF[:, j * 128:(j + 1) * 128]
                .rearrange("p (g t) -> p g t", g=4), AX.X, Alu.max),
                reads=lk, writes=[("mo", "mx")])
            p.op("dve", lambda v, j=j: v.tensor_tensor(
                C.PTB[:, j * 128:(j + 1) * 128].rearrange("p (g t) -> p g t", g=4),
                C.LOGF[:, j * 128:(j + 1) * 128].rearrange("p (g t) -> p g t", g=4),
                bass.AP(S, 56, [[256, 128], [1, 4], [0, 32]]), Alu.is_equal),
                reads=lk + [("mo", "mx")], writes=ak(C, C.PTB, j * 128, (j + 1) * 128),
                tag="mo_eq")
            if MOE_TRUNC == "lg":
                continue
            p.op("pe", lambda t, j=j: t.transpose(
                bb(C, 0), C.PTB[:, j * 128:(j + 1) * 128], C.ID[:, :]),
                reads=ak(C, C.PTB, j * 128, (j + 1) * 128),
                writes=[("PSB", 0)] + pbk(0), tag="mo_pt")
            p.op("dve", lambda v, j=j: v.tensor_copy(
                C.PTOK[:, 512:640], bb(C, 0)),
                reads=[("PSB", 0)] + pbk(0),
                writes=ak(C, C.PTOK, 512, 640))
            if MOE_TRUNC == "pt":
                continue
            for g in range(4):
                p.op("dve", lambda v, g=g: v.tensor_scalar(
                    C.PTOK[:, g * 128:(g + 1) * 128], C.PTOK[:, 512:640],
                    C.GMASK[:, g:g + 1], None, Alu.mult),
                    reads=ak(C, C.PTOK, 512, 640) + ak(C, C.GMASK, 0, 4),
                    writes=ak(C, C.PTOK, g * 128, (g + 1) * 128), tag="mo_pm")
            for c in range(8):
                b = 4 + (c % 2)
                p.op("pe", lambda t, j=j, c=c, b=b: t.matmul(
                    bk(C, b, 0, 512),
                    C.XLM[:, j * 1024 + c * 128: j * 1024 + (c + 1) * 128],
                    C.PTOK[:, 0:512], start=True, stop=True),
                    reads=ak(C, C.XLM, j * 1024 + c * 128, j * 1024 + (c + 1) * 128)
                    + ak(C, C.PTOK, 0, 512), writes=pk(b), tag="mo_dsp")
                def evd(v, j=j, c=c, b=b):
                    return v.tensor_copy(
                        bass.AP(C.DT, c * 4096 + j * 4,
                                [[32768, 128], [1, 4], [32, 128]]),
                        bk(C, b, 0, 512).rearrange("p (g e) -> p g e", g=4))
                p.op("dve", evd, reads=pk(b),
                     writes=ak(C, C.DT, c * 4096, (c + 1) * 4096), tag="mo_dte")
        if MOE_TRUNC in ("lg", "pt", "dsp"):
            r = slice(t0, t0 + 1024)
            p.dma("sp", dst[r, :].rearrange("(a p) d -> p a d", p=128),
                  C.XM[:, :].rearrange("p (a d) -> p a d", a=8),
                  reads=ak(C, C.XM, 0, 8192), writes=[("D", dst.name, "w", mi)],
                  tag="mo_xst")
            continue
        dtk = ak(C, C.DT, 0, 32768)
        for g16 in range(8):
            sl = C.S1A if g16 % 2 == 0 else C.S1B
            slk = ak(C, sl, 0, 4096)
            p.dma("pool", sl[:, :], C.d_in["F1"][g16, :, :], writes=slk,
                  tag="mo_f1ld")
            for e16 in range(16):
                es = g16 * 16 + e16
                er = es % 32
                bnk = 2 + (er // 16)
                def mm1(t, sl=sl, e16=e16, es=es, er=er):
                    last = None
                    for c in range(8):
                        last = t.matmul(
                            C.PS[0:32, 1024 + er * 32: 1024 + (er + 1) * 32],
                            sl[:, c * 512 + e16 * 32: c * 512 + (e16 + 1) * 32],
                            C.DT[:, c * 4096 + es * 32: c * 4096 + (es + 1) * 32],
                            start=(c == 0), stop=(c == 7),
                            skip_group_check=True)
                    return last
                p.op("pe", mm1, reads=slk + dtk, writes=pk(bnk), tag="mo_f1")
            if g16 % 2 == 1:
                for er in range(32):
                    es = (g16 // 2) * 32 + er
                    bnk = 2 + (er // 16)
                    p.op("act", lambda a, es=es, er=er: a.activation(
                        C.INR[0:32, es * 32:(es + 1) * 32],
                        C.PS[0:32, 1024 + er * 32: 1024 + (er + 1) * 32], AF.Relu,
                        bias=C.FB[0:32, es:es + 1]),
                        reads=pk(bnk) + ak(C, C.FB, es, es + 1),
                        writes=ak(C, C.INR, es * 32, (es + 1) * 32), tag="mo_relu")
        inrk = ak(C, C.INR, 0, 4096)
        if MOE_TRUNC == "f1":
            r = slice(t0, t0 + 1024)
            p.dma("sp", dst[r, :].rearrange("(a p) d -> p a d", p=128),
                  C.XM[:, :].rearrange("p (a d) -> p a d", a=8),
                  reads=ak(C, C.XM, 0, 8192), writes=[("D", dst.name, "w", mi)],
                  tag="mo_xst")
            continue
        s2k = ak(C, C.S2Z, 0, 16384)
        for c in range(8):
            for s2 in range(4):
                p.dma("pool", C.S2Z[0:32, s2 * 4096:(s2 + 1) * 4096],
                      C.d_in["F2"][c, s2 * 32:(s2 + 1) * 32, :],
                      reads=[], writes=ak(C, C.S2Z, s2 * 4096, (s2 + 1) * 4096),
                      tag="mo_f2ld")
            ib = C.S1A if c % 2 == 0 else C.S1B
            ibk = ak(C, ib, 0, 4096)
            for half in range(8):
                b = 4 + (half % 2)
                def mm2(t, c=c, half=half, b=b):
                    last = None
                    for i in range(16):
                        es = half * 16 + i
                        e, s2 = es // 4, es % 4
                        last = t.matmul(
                            C.PS[:, 512 * b + i * 32: 512 * b + i * 32 + 32],
                            C.S2Z[0:32, s2 * 4096 + e * 128: s2 * 4096 + (e + 1) * 128],
                            C.INR[0:32, es * 32:(es + 1) * 32],
                            start=True, stop=True)
                    return last
                p.op("pe", mm2, reads=s2k + inrk, writes=pk(b), tag="mo_f2")
                p.op("dve", lambda v, c=c, half=half, b=b, ib=ib: v.tensor_copy(
                    ib[:, half * 512:(half + 1) * 512], bk(C, b)),
                    reads=pk(b), writes=ak(C, ib, half * 512, (half + 1) * 512),
                    tag="mo_ibe")
            for g in range(32):
                sl2 = g % 8
                def trg(t, sl2=sl2, ib=ib, g=g):
                    return t.transpose(
                        bb(C, sl2), bass.AP(ib, g, [[4096, 128], [32, 128]]),
                        C.ID[:, :])
                p.op("pe", trg, reads=ak(C, ib, 0, 4096),
                     writes=[("PSB", sl2)] + pbk(sl2), tag="mo_it")
                p.op("dve", lambda v, c=c, g=g, sl2=sl2: v.tensor_copy(
                    C.DT[:, g * 1024 + c * 128: g * 1024 + (c + 1) * 128], bb(C, sl2)),
                    reads=[("PSB", sl2)] + pbk(sl2),
                    writes=ak(C, C.DT, g * 1024 + c * 128, g * 1024 + (c + 1) * 128),
                    tag="mo_ite")
        if MOE_TRUNC == "f2":
            r = slice(t0, t0 + 1024)
            p.dma("sp", dst[r, :].rearrange("(a p) d -> p a d", p=128),
                  C.XM[:, :].rearrange("p (a d) -> p a d", a=8),
                  reads=ak(C, C.XM, 0, 8192), writes=[("D", dst.name, "w", mi)],
                  tag="mo_xst")
            continue
        for j in range(8):
            for n in range(2):
                b = n
                def mms(t, j=j, n=n, b=b):
                    last = None
                    for gg in range(4):
                        g = j * 4 + gg
                        last = t.matmul(
                            C.PS[gg * 32:(gg + 1) * 32, 512 * b: 512 * (b + 1)],
                            C.PTB[:, j * 128 + gg * 32: j * 128 + gg * 32 + 32],
                            C.DT[:, g * 1024 + n * 512: g * 1024 + (n + 1) * 512],
                            start=True, stop=True,
                            tile_position=(0, gg * 32))
                    return last
                p.op("pe", mms, reads=ak(C, C.PTB, j * 128, (j + 1) * 128)
                     + ak(C, C.DT, j * 4096, (j + 1) * 4096),
                     writes=pk(b), tag="mo_sc")
                xsk = ak(C, C.XM, j * 1024 + n * 512, j * 1024 + (n + 1) * 512)
                p.op("dve", lambda v, j=j, n=n, b=b: v.scalar_tensor_tensor(
                    C.XM[:, j * 1024 + n * 512: j * 1024 + (n + 1) * 512],
                    bk(C, b), 0.5, C.XM[:, j * 1024 + n * 512: j * 1024 + (n + 1) * 512],
                    Alu.mult, Alu.add),
                    reads=pk(b) + xsk, writes=xsk, tag="mo_res")
        r = slice(t0, t0 + 1024)
        p.dma("sp", dst[r, :].rearrange("(a p) d -> p a d", p=128),
              C.XM[:, :].rearrange("p (a d) -> p a d", a=8),
              reads=ak(C, C.XM, 0, 8192), writes=[("D", dst.name, "w", mi)],
              tag="mo_xst")


def emit_hypattn(C, src, dst):
    p, cv, S, A4 = C.p, C.cva, C.SMALL, C.AUX4
    hd = C.HD
    for s in range(NSEQ):
        wqk = ak(C, C.WQKV, 0, 24576)
        p.dma("pool", C.WQKV[:, :].rearrange("p (c f) -> p c f", c=8),
              C.d_in["WqkvT0"][:, :].rearrange("(c p) f -> p c f", p=128),
              writes=wqk, tag="ha_wq")
        p.op("dve", lambda v: v.tensor_copy(
            bass.AP(C.VT, 64, [[16640, 128], [1040, 16], [65, 16]]),
            bass.AP(C.ONES, 0, [[128, 128], [0, 16], [0, 16]])),
            reads=ak(C, C.ONES, 0, 128), writes=[("ha", "vones")], tag="ha_vones")
        for mini in range(4):
            t0 = s * SEQ + mini * 512
            emit_ln(C, src, t0, 4, C.AXW, C.AXL, C.SQA, f"ha{s}{mini}")
            for j in range(4):
                tile = mini * 4 + j
                for c in range(8):
                    sl = c % 8
                    p.op("pe", lambda t, j=j, c=c, sl=sl: t.transpose(
                        bb(C, sl), C.AXL[:, j * 1024 + c * 128: j * 1024 + (c + 1) * 128],
                        C.ID[:, :]),
                        reads=ak(C, C.AXL, j * 1024 + c * 128, j * 1024 + (c + 1) * 128),
                        writes=[("PSB", sl)] + pbk(sl), tag="ha_xt")
                    p.op("dve", lambda v, c=c, sl=sl: v.tensor_copy(
                        C.AXT[:, c * 128:(c + 1) * 128], bb(C, sl)),
                        reads=[("PSB", sl)] + pbk(sl),
                        writes=ak(C, C.AXT, c * 128, (c + 1) * 128))
                xtk = ak(C, C.AXT, 0, 1024)
                for n in range(6):
                    b = 3 + (n % 2)
                    def mmqkv(t, n=n, b=b):
                        last = None
                        for c in range(8):
                            last = t.matmul(bk(C, b), C.AXT[:, c * 128:(c + 1) * 128],
                                            C.WQKV[:, c * 3072 + n * 512: c * 3072 + (n + 1) * 512],
                                            start=(c == 0), stop=(c == 7))
                        return last
                    p.op("pe", mmqkv, reads=xtk + wqk, writes=pk(b), tag="ha_qkv")
                    if n < 2:
                        dstt, off = C.QTM, n * 512
                        wk = ak(C, C.QTM, off, off + 512)
                        p.op("dve", lambda v, b=b, off=off: v.tensor_copy(
                            C.QTM[:, off:off + 512], bk(C, b)), reads=pk(b), writes=wk,
                            tag="ha_eq")
                    elif n < 4:
                        off = (n - 2) * 512
                        wk = ak(C, C.KTM, off, off + 512)
                        p.op("dve", lambda v, b=b, off=off: v.tensor_copy(
                            C.KTM[:, off:off + 512], bk(C, b)), reads=pk(b), writes=wk,
                            tag="ha_ek")
                    else:
                        off = (n - 4) * 8
                        def evv(v, b=b, tile=tile, off=off):
                            return v.tensor_copy(
                                bass.AP(C.VT, tile * 1040 + off * 65,
                                        [[16640, 128], [65, 8], [1, 64]]),
                                bk(C, b).rearrange("p (h d) -> p h d", h=8))
                        p.op("dve", evv, reads=pk(b),
                             writes=ak(C, C.VT, tile * 1040, (tile + 1) * 1040),
                             tag="ha_ev")
                if DEBUG_HA and s == 0 and mini == 0 and j == 0:
                    p.dma("pool", C.DBG[:, 6144:7168],
                          C.QTM[:, :], reads=ak(C, C.QTM, 0, 1024),
                          writes=[("D", "DBG", 5)], tag="dbg6")
                    p.dma("pool", C.DBG[:, 7168:8192],
                          C.AXL[:, 0:1024], reads=ak(C, C.AXL, 0, 1024),
                          writes=[("D", "DBG", 6)], tag="dbg7")
                for (tm, tag) in ((C.QTM, "q"), (C.KTM, "k")):
                    tk = ak(C, tm, 0, 1024)
                    a1 = bass.AP(tm, 0, [[1024, 128], [64, 16], [1, 32]])
                    a2 = bass.AP(tm, 32, [[1024, 128], [64, 16], [1, 32]])
                    cb = bass.AP(C.RC, tile * 32, [[512, 128], [0, 16], [1, 32]])
                    sb2 = bass.AP(C.RS, tile * 32, [[512, 128], [0, 16], [1, 32]])
                    r3 = lambda lo: bass.AP(C.RSC, lo, [[2048, 128], [32, 16], [1, 32]])
                    rck = ak(C, C.RC, 0, 512) + ak(C, C.RS, 0, 512)
                    rk0 = ak(C, C.RSC, 0, 512)
                    rk1 = ak(C, C.RSC, 512, 1024)
                    rk2 = ak(C, C.RSC, 1024, 1536)
                    p.op("dve", lambda v, a1=a1, cb=cb: v.tensor_tensor(r3(0), a1, cb, Alu.mult),
                         reads=tk + rck, writes=rk0, tag="rp1" + tag)
                    p.op("dve", lambda v, a2=a2, sb2=sb2: v.tensor_tensor(r3(512), a2, sb2, Alu.mult),
                         reads=tk + rck, writes=rk1, tag="rp2" + tag)
                    p.op("dve", lambda v, a1=a1, sb2=sb2: v.tensor_tensor(r3(1024), a1, sb2, Alu.mult),
                         reads=tk + rck, writes=rk2, tag="rp3" + tag)
                    p.op("dve", lambda v, a2=a2, cb=cb: v.tensor_tensor(a2, a2, cb, Alu.mult),
                         reads=tk + rck, writes=tk, tag="rp4" + tag)
                    p.op("dve", lambda v, a2=a2: v.tensor_tensor(a2, a2, r3(1024), Alu.subtract),
                         reads=tk + rk2, writes=tk, tag="rp5" + tag)
                    p.op("dve", lambda v, a1=a1: v.tensor_tensor(a1, r3(0), r3(512), Alu.add),
                         reads=rk0 + rk1, writes=tk, tag="rp6" + tag)
                    if DEBUG_HA and s == 0 and mini == 0 and j == 0 and tag == "q":
                        p.dma("pool", C.DBG[:, 0:1024] if False else C.DBG[:, 6144:7168],
                              tm[:, :], reads=tk, writes=[("D", "DBG", 5)],
                              tag="dbg8")
                    sqoff = 0 if tag == "q" else 256
                    p.op("act", lambda a, tm=tm: a.activation(C.RSC[:, 0:1024], tm[:, :], AF.Square),
                         reads=tk, writes=rk0 + rk1, tag="rpsq" + tag)
                    p.op("dve", lambda v, tile=tile, sqoff=sqoff: v.tensor_reduce(
                        A4[:, 256 + sqoff + tile * 16: 256 + sqoff + (tile + 1) * 16],
                        C.RSC[:, 0:1024].rearrange("p (h d) -> p h d", h=16),
                        AX.X, Alu.add),
                        reads=rk0 + rk1, writes=[("sq" + tag, tile)], tag="rsq" + tag)
                    dstf = C.QF if tag == "q" else C.KF
                    for pb in range(8):
                        sl = pb % 8
                        p.op("pe", lambda t, tm=tm, pb=pb, sl=sl: t.transpose(
                            bb(C, sl), tm[:, pb * 128:(pb + 1) * 128], C.ID[:, :]),
                            reads=tk, writes=[("PSB", sl)] + pbk(sl), tag="ft" + tag)
                        if tag == "q":
                            def fev(v, dstf=dstf, pb=pb, tile=tile, sl=sl):
                                return v.tensor_copy(
                                    dstf[:, pb * 2048 + tile * 128: pb * 2048 + (tile + 1) * 128],
                                    bb(C, sl))
                        else:
                            def fev(v, dstf=dstf, pb=pb, tile=tile, sl=sl):
                                return v.tensor_scalar(
                                    dstf[:, pb * 2048 + tile * 128: pb * 2048 + (tile + 1) * 128],
                                    bb(C, sl), -1.0, None, Alu.mult)
                        p.op("dve", fev,
                            reads=[("PSB", sl)] + pbk(sl),
                            writes=ak(C, dstf, pb * 2048 + tile * 128, pb * 2048 + (tile + 1) * 128))
        for (sqoff, outoff, tg) in ((256, 768, "q"), (512, 512, "k")):
            rds = [("sq" + tg, t) for t in range(16)]
            kcb = bass.AP(C.HYP, 0, [[16, 128], [0, 16], [1, 16]])
            p.op("dve", lambda v, sqoff=sqoff, kcb=kcb: v.tensor_tensor(
                A4[:, sqoff:sqoff + 256].rearrange("p (t h) -> p t h", t=16),
                A4[:, sqoff:sqoff + 256].rearrange("p (t h) -> p t h", t=16),
                kcb, Alu.add),
                reads=rds + ak(C, C.HYP, 0, 16), writes=rds, tag="k0a" + tg)
            p.op("act", lambda a, sqoff=sqoff: a.activation(
                A4[:, sqoff:sqoff + 256], A4[:, sqoff:sqoff + 256], AF.Ln),
                reads=rds, writes=rds, tag="k0b" + tg)
            p.op("act", lambda a, sqoff=sqoff, outoff=outoff: a.activation(
                A4[:, outoff:outoff + 256] if outoff != sqoff else A4[:, sqoff:sqoff + 256],
                A4[:, sqoff:sqoff + 256], AF.Exp, scale=cv["half"]),
                reads=rds, writes=[("c0", tg)], tag="k0c" + tg)
        for t16 in range(16):
            for (srcoff, dstt, kk, ck) in ((768, C.QROWS, "qrow", "q"),
                                           (512, C.KROWS, "krow", "k")):
                p.op("pe", lambda t, srcoff=srcoff, t16=t16: t.transpose(
                    C.PS[0:16, 2560:2688],
                    A4[:, srcoff + t16 * 16: srcoff + (t16 + 1) * 16],
                    C.IDF[:, :]),
                    reads=[("c0", ck)] + ak(C, C.IDF, 0, 128),
                    writes=[("q0p",)] + pk(5), tag="qkt")
                p.op("dve", lambda v, dstt=dstt, t16=t16: v.tensor_copy(
                    dstt[0:16, t16 * 128:(t16 + 1) * 128], C.PS[0:16, 2560:2688]),
                    reads=[("q0p",)] + pk(5), writes=[(kk, t16)], tag="qke")
        C.avcnt = 0
        for h in range(16):
            p.dma("sp", C.QRH[0:1, :], C.QROWS[h:h + 1, :],
                  reads=[("qrow", t) for t in range(16)], writes=[("qkr", "q")],
                  tag="qrh")
            p.dma("sp", C.KRH[0:1, :], C.KROWS[h:h + 1, :],
                  reads=[("krow", t) for t in range(16)], writes=[("qkr", "k")],
                  tag="krh")
            for kc in range(16):
                qlo = kc * 128
                pieces = []
                q0 = qlo
                while q0 < 2048:
                    w = min(1024, 2048 - q0)
                    pieces.append((q0, w))
                    q0 += w
                for (pq0, w) in pieces:
                    nsl = (w + 511) // 512
                    def mmsc2(t, h=h, kc=kc, pq0=pq0, w=w, nsl=nsl, qlo=qlo):
                        hh = h % 2
                        pb = h // 2
                        last = None
                        for sl2 in range(nsl):
                            wn = min(512, w - sl2 * 512)
                            ob = 1536 + sl2 * 512
                            kcb = pb * 2048 + kc * 128
                            qb = pb * 2048 + pq0 + sl2 * 512
                            rb = pq0 + sl2 * 512
                            if hh == 0:
                                t.matmul(
                                    C.PS[:, ob: ob + wn],
                                    C.KF[0:64, kcb: kcb + 128],
                                    C.QF[0:64, qb: qb + wn],
                                    start=True, stop=False,
                                    tile_position=(0, 0), skip_group_check=True)
                                last = t.matmul(
                                    C.PS[:, ob: ob + wn],
                                    C.KRH[0:1, kc * 128:(kc + 1) * 128],
                                    C.QRH[0:1, rb: rb + wn],
                                    start=False, stop=True, skip_group_check=True)
                            else:
                                for mh in range(2):
                                    t.matmul(
                                        C.PS[mh * 64:(mh + 1) * 64, ob: ob + wn],
                                        C.KF[64:128, kcb + mh * 64: kcb + mh * 64 + 64],
                                        C.QF[64:128, qb: qb + wn],
                                        start=True, stop=False,
                                        tile_position=(64, mh * 64),
                                        skip_group_check=True)
                                    last = t.matmul(
                                        C.PS[mh * 64:(mh + 1) * 64, ob: ob + wn],
                                        C.KRH[0:1, kc * 128 + mh * 64: kc * 128 + mh * 64 + 64],
                                        C.QRH[0:1, rb: rb + wn],
                                        start=False, stop=True,
                                        tile_position=(0, mh * 64),
                                        skip_group_check=True)
                        return last
                    p.op("pe", mmsc2,
                         reads=ak(C, C.KF, 0, 16384) + ak(C, C.QF, 0, 16384)
                         + [("qkr", "q"), ("qkr", "k")],
                         writes=pk(3) + pk(4), tag="ha_sc")
                    p.op("dve", lambda v, w=w, h=h: v.tensor_scalar(
                        C.B2[:, 0:w], C.PS[:, 1536:1536 + w], hd[:, h:h + 1], 1.0,
                        Alu.mult, Alu.max),
                        reads=pk(3) + pk(4) + [("hd", "rinv")], writes=[("ch", 2)],
                        tag="ha_v2")
                    p.op("dve", lambda v, w=w: v.tensor_tensor(
                        C.B1[:, 0:w], C.B2[:, 0:w], C.B2[:, 0:w], Alu.mult),
                        reads=[("ch", 2)], writes=[("ch", 1)], tag="ha_v3")
                    p.op("act", lambda a, w=w: a.activation(
                        C.B1[:, 0:w], C.B1[:, 0:w], AF.Ln, bias=cv["epsa1"]),
                        reads=[("ch", 1)], writes=[("ch", 1)], tag="ha_s1")
                    p.op("act", lambda a, w=w: a.activation(
                        C.B1[:, 0:w], C.B1[:, 0:w], AF.Exp, scale=cv["half"]),
                        reads=[("ch", 1)], writes=[("ch", 1)], tag="ha_s2")
                    p.op("dve", lambda v, w=w: v.tensor_tensor(
                        C.B2[:, 0:w], C.B2[:, 0:w], C.B1[:, 0:w], Alu.add),
                        reads=[("ch", 1), ("ch", 2)], writes=[("ch", 2)], tag="ha_v4")
                    p.op("act", lambda a, w=w: a.activation(
                        C.B2[:, 0:w], C.B2[:, 0:w], AF.Ln),
                        reads=[("ch", 2)], writes=[("ch", 2)], tag="ha_s3")
                    p.op("act", lambda a, w=w, h=h: a.activation(
                        C.BF[:, 0:w], C.B2[:, 0:w], AF.Ln,
                        scale=hd[:, 16 + h:17 + h], bias=cv["eps6"]),
                        reads=[("ch", 2), ("hd", "sig")], writes=[("ch", 4)], tag="ha_s4")
                    p.op("act", lambda a, w=w: a.activation(
                        C.BF[:, 0:w], C.BF[:, 0:w], AF.Exp, scale=cv["negone"]),
                        reads=[("ch", 4)], writes=[("ch", 4)], tag="ha_s5")
                    p.op("act", lambda a, w=w: a.activation(
                        C.ET[:, 0:w], C.BF[:, 0:w], AF.Exp),
                        reads=[("ch", 4)], writes=[("ch", 3)], tag="ha_s6")
                    if pq0 == qlo:
                        p.op("dve", lambda v: v.tensor_tensor(
                            C.ET[:, 0:128], C.ET[:, 0:128], C.MD[:, :], Alu.mult),
                            reads=[("ch", 3)] + ak(C, C.MD, 0, 128),
                            writes=[("ch", 3)], tag="ha_msk")
                    if DEBUG_HA and s == 0 and h == 0 and kc == 0 and pq0 == 0:
                        p.dma("pool", C.DBG[:, 0:1024],
                              bass.AP(C.ET, 0, [[2048, 128], [1, 1024]]),
                              reads=[("ch", 3)], writes=[("D", "DBG", 0)], tag="dbg1")
                        p.dma("sp", C.DBG[:, 2048:2816], A4[:, 256:1024],
                              reads=[("c0", "k"), ("c0", "q")], writes=[("D", "DBG", 2)],
                              tag="dbg3")
                        p.dma("pool", C.DBG[:, 4096:6144],
                              C.QF[:, 0:2048], reads=ak(C, C.QF, 0, 2048),
                              writes=[("D", "DBG", 4)], tag="dbg5")
                        p.dma("pool", C.DBG[:, 3072:4096],
                              bass.AP(C.B2, 0, [[2048, 128], [1, 1024]]),
                              reads=[("ch", 2)], writes=[("D", "DBG", 3)], tag="dbg4")
                    for i in range(w // 128):
                        qt = (pq0 + i * 128) // 128
                        bnk = C.avcnt % 3
                        C.avcnt += 1
                        def mmv(t, h=h, kc=kc, i=i, bnk=bnk):
                            return t.matmul(
                                C.PS[:, bnk * 512: bnk * 512 + 65],
                                C.ET[:, i * 128:(i + 1) * 128],
                                C.VT[:, kc * 1040 + h * 65: kc * 1040 + (h + 1) * 65],
                                start=True, stop=True,
                                skip_group_check=True)
                        p.op("pe", mmv, reads=[("ch", 3), ("ha", "vones")]
                             + ak(C, C.VT, kc * 1040, (kc + 1) * 1040),
                             writes=pk(bnk), tag="ha_av")
                        if kc == 0:
                            p.op("dve", lambda v, qt=qt, bnk=bnk: v.tensor_copy(
                                C.AOC[:, qt * 65:(qt + 1) * 65],
                                C.PS[:, bnk * 512: bnk * 512 + 65]),
                                reads=pk(bnk),
                                writes=[("aoc", qt)] + ak(C, C.AOC, qt * 65, (qt + 1) * 65),
                                tag="ha_ac0")
                        else:
                            p.op("dve", lambda v, qt=qt, bnk=bnk: v.tensor_tensor(
                                C.AOC[:, qt * 65:(qt + 1) * 65],
                                C.PS[:, bnk * 512: bnk * 512 + 65],
                                C.AOC[:, qt * 65:(qt + 1) * 65], Alu.add),
                                reads=pk(bnk) + [("aoc", qt)],
                                writes=[("aoc", qt)] + ak(C, C.AOC, qt * 65, (qt + 1) * 65),
                                tag="ha_aca")
            for qt in range(16):
                p.op("dve", lambda v, qt=qt: v.reciprocal(
                    S[:, 64 + qt:65 + qt], C.AOC[:, qt * 65 + 64: qt * 65 + 65]),
                    reads=[("aoc", qt)], writes=[("nr", qt)], tag="ha_rc")
                p.op("dve", lambda v, qt=qt, h=h: v.tensor_scalar(
                    C.AO[:, qt * 1024 + h * 64: qt * 1024 + (h + 1) * 64],
                    C.AOC[:, qt * 65: qt * 65 + 64], S[:, 64 + qt:65 + qt], None,
                    Alu.mult),
                    reads=[("aoc", qt), ("nr", qt)],
                    writes=ak(C, C.AO, qt * 1024 + h * 64, qt * 1024 + (h + 1) * 64),
                    tag="ha_nm")
        wok = ak(C, C.WOUT, 0, 8192)
        p.dma("pool", C.WOUT[:, :].rearrange("p (c f) -> p c f", c=8),
              C.d_in["WoutT0"][:, :].rearrange("(c p) f -> p c f", p=128),
              writes=wok + [("qkr", "q"), ("qkr", "k")], tag="ha_wo")
        for mini in range(4):
            t0 = s * SEQ + mini * 512
            r = slice(t0, t0 + 512)
            xwk = ak(C, C.AXW, 0, 4096)
            p.dma("sp", C.AXW[:, :].rearrange("p (a d) -> p a d", a=4),
                  src[r, :].rearrange("(a p) d -> p a d", p=128),
                  reads=[("D", src.name, "c", t0)], writes=xwk, tag="ha_xld")
            for j in range(4):
                tile = mini * 4 + j
                for c in range(8):
                    sl = c % 8
                    p.op("pe", lambda t, tile=tile, c=c, sl=sl: t.transpose(
                        bb(C, sl), C.AO[:, tile * 1024 + c * 128: tile * 1024 + (c + 1) * 128],
                        C.ID[:, :]),
                        reads=ak(C, C.AO, tile * 1024 + c * 128, tile * 1024 + (c + 1) * 128),
                        writes=[("PSB", sl)] + pbk(sl), tag="ha_aot")
                    p.op("dve", lambda v, c=c, sl=sl: v.tensor_copy(
                        C.AXT[:, c * 128:(c + 1) * 128], bb(C, sl)),
                        reads=[("PSB", sl)] + pbk(sl),
                        writes=ak(C, C.AXT, c * 128, (c + 1) * 128))
                for n in range(2):
                    b = n
                    def mmo(t, n=n, b=b):
                        last = None
                        for c in range(8):
                            last = t.matmul(bk(C, b), C.AXT[:, c * 128:(c + 1) * 128],
                                            C.WOUT[:, c * 1024 + n * 512: c * 1024 + (n + 1) * 512],
                                            start=(c == 0), stop=(c == 7))
                        return last
                    p.op("pe", mmo, reads=ak(C, C.AXT, 0, 1024) + wok, writes=pk(b),
                         tag="ha_po")
                    xsk = ak(C, C.AXW, j * 1024 + n * 512, j * 1024 + (n + 1) * 512)
                    p.op("dve", lambda v, j=j, n=n, b=b: v.tensor_tensor(
                        C.AXW[:, j * 1024 + n * 512: j * 1024 + (n + 1) * 512],
                        C.AXW[:, j * 1024 + n * 512: j * 1024 + (n + 1) * 512],
                        bk(C, b), Alu.add),
                        reads=pk(b) + xsk, writes=xsk, tag="ha_res")
            p.dma("sp", dst[r, :].rearrange("(a p) d -> p a d", p=128),
                  C.AXW[:, :].rearrange("p (a d) -> p a d", a=4),
                  reads=xwk, writes=[("D", dst.name, "w2", t0)], tag="ha_xst")
